# version 05 (final)
# baseline (speedup 1.0000x reference)
import numpy as np
import ml_dtypes
from contextlib import ExitStack
import concourse.bass as bass
import concourse.mybir as mybir
from concourse.bass_utils import run_bass_kernel_spmd

F32 = mybir.dt.float32
BF16 = mybir.dt.bfloat16
I32 = mybir.dt.int32
AF = mybir.ActivationFunctionType
ALU = mybir.AluOpType
AX = mybir.AxisListType

NC = 8
B, L, D = 4, 2048, 2048
NH, DK, DV, KC = 16, 128, 128, 4
C = 128
NCH = L // C
BIG = 1.0e6
INF = 1.0e30
ENGS = ["sync", "vector", "scalar", "tensor", "gpsimd"]

BF = ml_dtypes.bfloat16


class Prog:

    def __init__(self, nc):
        self.nc = nc
        self.items = []

    def op(self, eng, meth, *a, **k):
        self.items.append((eng, lambda E, m=meth, a=a, k=k: getattr(E, m)(*a, **k), 1))

    def dma(self, out, in_):
        self.items.append(("sync", lambda E, o=out, i=in_: E.dma_start(out=o, in_=i), 16))

    def finalize(self, st):
        nc = self.nc
        sems = {e: st.enter_context(nc.semaphore("sem_" + e)) for e in ENGS}
        block = st.enter_context(nc.Block())
        cnt = {e: 0 for e in ENGS}
        lastw = {e: {o: 0 for o in ENGS} for e in ENGS}
        waits = []
        for (e, fn, inc) in self.items:
            wl = []
            for o in ENGS:
                if cnt[o] > lastw[e][o]:
                    wl.append((o, cnt[o]))
                    lastw[e][o] = cnt[o]
            waits.append(wl)
            cnt[e] += inc

        items = self.items

        def emit(E, name):
            for (e, fn, inc), wl in zip(items, waits):
                if e != name:
                    continue
                for (o, tgt) in wl:
                    E.wait_ge(sems[o], tgt)
                fn(E).then_inc(sems[e], inc)
            if name == "scalar":
                for o in ENGS:
                    if cnt[o] > 0:
                        E.wait_ge(sems[o], cnt[o])

        block.sync(lambda E: emit(E, "sync"))
        block.vector(lambda E: emit(E, "vector"))
        block.scalar(lambda E: emit(E, "scalar"))
        block.tensor(lambda E: emit(E, "tensor"))
        block.gpsimd(lambda E: emit(E, "gpsimd"))


def _round_to_int(p, tmp_f32_ap, iti_ap):
    p.op("vector", "tensor_copy", out=iti_ap, in_=tmp_f32_ap)
    p.op("vector", "tensor_copy", out=tmp_f32_ap, in_=iti_ap)


def _emit_plain_ln(p, src_ap, dst_ap, sq_ap, scol, mucol, vcol, sdcol, rscol, Dfree):
    p.op("vector", "reduce_sum", out=scol, in_=src_ap, axis=AX.X)
    p.op("vector", "tensor_scalar_mul", out=mucol, in0=scol, scalar1=1.0 / Dfree)
    p.op("vector", "tensor_scalar_sub", out=dst_ap, in0=src_ap, scalar1=mucol)
    p.op("scalar", "activation", out=sq_ap, in_=dst_ap, func=AF.Square)
    p.op("vector", "reduce_sum", out=vcol, in_=sq_ap, axis=AX.X)
    p.op("vector", "tensor_scalar", out=vcol, in0=vcol, scalar1=1.0 / Dfree, scalar2=1e-5,
         op0=ALU.mult, op1=ALU.add)
    p.op("scalar", "activation", out=sdcol, in_=vcol, func=AF.Sqrt)
    p.op("vector", "reciprocal", out=rscol, in_=sdcol)
    p.op("vector", "tensor_scalar_mul", out=dst_ap, in0=dst_ap, scalar1=rscol)


def build_l1():
    nc = bass.Bass("TRN2", target_bir_lowering=False)
    x = nc.dram_tensor("x", [L, D], F32, kind="ExternalInput").ap()
    wq = nc.dram_tensor("wq", [D, 1024], F32, kind="ExternalInput").ap()
    wk = nc.dram_tensor("wk", [D, 1024], F32, kind="ExternalInput").ap()
    wv = nc.dram_tensor("wv", [D, 1024], F32, kind="ExternalInput").ap()
    wg = nc.dram_tensor("wg", [D, 1024], F32, kind="ExternalInput").ap()
    wa = nc.dram_tensor("wa", [D, 8], F32, kind="ExternalInput").ap()
    wbt = nc.dram_tensor("wbt", [D, 8], F32, kind="ExternalInput").ap()
    bab = nc.dram_tensor("bab", [2, 8, 1], F32, kind="ExternalInput").ap()
    cw = nc.dram_tensor("cw", [3, 8, 128, 4], F32, kind="ExternalInput").ap()
    cb = nc.dram_tensor("cb", [3, 8, 128, 1], F32, kind="ExternalInput").ap()
    ident = nc.dram_tensor("ident", [128, 128], F32, kind="ExternalInput").ap()
    xn_out = nc.dram_tensor("xn_out", [L, D], F32, kind="ExternalOutput").ap()
    qcT = nc.dram_tensor("qcT", [1024, L], F32, kind="ExternalOutput").ap()
    kcT = nc.dram_tensor("kcT", [1024, L], F32, kind="ExternalOutput").ap()
    vcT = nc.dram_tensor("vcT", [1024, L], F32, kind="ExternalOutput").ap()
    gT = nc.dram_tensor("gT", [1024, L], F32, kind="ExternalOutput").ap()
    ab = nc.dram_tensor("ab", [2, 8, L], F32, kind="ExternalOutput").ap()

    st = ExitStack()
    T = lambda n, s, d=F32: st.enter_context(nc.sbuf_tensor(n, s, d))
    xsb = T("xsb", [128, D])
    xnb = T("xnb", [128, D])
    sq = T("sq", [128, D])
    xnT = T("xnT", [128, 16 * D], BF16)
    wraw = T("wraw", [128, 16 * 128])
    wbf = T("wbf", [128, 16 * 128], BF16)
    qext = T("qext", [128, 3 + L])
    accA = T("accA", [128, L])
    accB = T("accB", [128, L])
    outbf = T("outbf", [128, L])
    cwt = T("cwt", [128, 4])
    cbt = T("cbt", [128, 1])
    wab = T("wab", [128, 2 * 128])
    wab_bf = T("wab_bf", [128, 2 * 128], BF16)
    ba_sb = T("ba_sb", [8, 2])
    absb = T("absb", [8, L])
    ids = T("ids", [128, 128])
    scol = T("scol", [128, 1]); mucol = T("mucol", [128, 1])
    vcol = T("vcol", [128, 1]); sdcol = T("sdcol", [128, 1]); rscol = T("rscol", [128, 1])
    ps_tr = st.enter_context(nc.psum_tensor("ps_tr", [128, 128], F32))
    ps_mm = st.enter_context(nc.psum_tensor("ps_mm", [128, 512], F32))

    p = Prog(nc)
    p.dma(ids[:], ident[:])
    p.dma(wab[:, 0:128].rearrange("p (db c) -> p db c", db=16),
          wa.rearrange("(db p) c -> p db c", p=128))
    p.dma(wab[:, 128:256].rearrange("p (db c) -> p db c", db=16),
          wbt.rearrange("(db p) c -> p db c", p=128))
    p.dma(ba_sb[:, 0:1], bab[0])
    p.dma(ba_sb[:, 1:2], bab[1])
    p.op("scalar", "copy", out=wab_bf[:], in_=wab[:])

    for tb in range(16):
        p.dma(xsb[:], x[tb * 128:(tb + 1) * 128, :])
        _emit_plain_ln(p, xsb[:], xnb[:], sq[:], scol[:], mucol[:], vcol[:], sdcol[:], rscol[:], D)
        p.dma(xn_out[tb * 128:(tb + 1) * 128, :], xnb[:])
        for db in range(16):
            p.op("tensor", "transpose", out=ps_tr[:], in_=xnb[:, db * 128:(db + 1) * 128], identity=ids[:])
            p.op("scalar", "copy", out=xnT[:, db * D + tb * 128: db * D + tb * 128 + 128], in_=ps_tr[:])

    for j, outrow in [(0, 0), (1, 1)]:
        for ch in range(4):
            for db in range(16):
                p.op("tensor", "matmul", out=ps_mm[0:8, 0:512],
                     lhsT=wab_bf[:, j * 128 + db * 8: j * 128 + db * 8 + 8],
                     rhs=xnT[:, db * D + ch * 512: db * D + ch * 512 + 512],
                     start=(db == 0), stop=(db == 15))
            p.op("scalar", "activation", out=absb[0:8, ch * 512:(ch + 1) * 512], in_=ps_mm[0:8, 0:512],
                 func=AF.Sigmoid, bias=ba_sb[0:8, j:j + 1])
        p.dma(ab[outrow], absb[0:8, :])

    projs = [(wq, qcT, 0, True, 1.0), (wk, kcT, 1, True, DK ** -0.5),
             (wv, vcT, 2, True, 1.0), (wg, gT, -1, False, 1.0)]
    for (wdram, odram, ci, has_conv, kscale) in projs:
        wview = wdram.rearrange("(db p) c -> p db c", p=128)
        for ob in range(8):
            p.dma(wraw[:].rearrange("p (db c) -> p db c", db=16), wview[:, :, ob * 128:(ob + 1) * 128])
            p.op("scalar", "copy", out=wbf[:], in_=wraw[:])
            if has_conv:
                p.dma(cwt[:], cw[ci, ob])
                p.dma(cbt[:], cb[ci, ob])
                p.op("vector", "memset", qext[:, 0:3], 0.0)
            for ch in range(4):
                for db in range(16):
                    p.op("tensor", "matmul", out=ps_mm[:, 0:512],
                         lhsT=wbf[:, db * 128:(db + 1) * 128],
                         rhs=xnT[:, db * D + ch * 512: db * D + ch * 512 + 512],
                         start=(db == 0), stop=(db == 15))
                if has_conv:
                    p.op("scalar", "copy", out=qext[:, 3 + ch * 512: 3 + ch * 512 + 512], in_=ps_mm[:, 0:512])
                else:
                    p.op("scalar", "activation", out=outbf[:, ch * 512:(ch + 1) * 512], in_=ps_mm[:, 0:512],
                         func=AF.Sigmoid)
            if has_conv:
                p.op("vector", "tensor_scalar_mul", out=accA[:], in0=qext[:, 0:L], scalar1=cwt[:, 0:1])
                p.op("vector", "scalar_tensor_tensor", out=accB[:], in0=qext[:, 1:1 + L], scalar=cwt[:, 1:2],
                     in1=accA[:], op0=ALU.mult, op1=ALU.add)
                p.op("vector", "scalar_tensor_tensor", out=accA[:], in0=qext[:, 2:2 + L], scalar=cwt[:, 2:3],
                     in1=accB[:], op0=ALU.mult, op1=ALU.add)
                p.op("vector", "scalar_tensor_tensor", out=accB[:], in0=qext[:, 3:3 + L], scalar=cwt[:, 3:4],
                     in1=accA[:], op0=ALU.mult, op1=ALU.add)
                if kscale != 1.0:
                    p.op("scalar", "activation", out=accA[:], in_=accB[:], func=AF.Silu, bias=cbt[:, 0:1])
                    p.op("vector", "tensor_scalar_mul", out=outbf[:], in0=accA[:], scalar1=kscale)
                else:
                    p.op("scalar", "activation", out=outbf[:], in_=accB[:], func=AF.Silu, bias=cbt[:, 0:1])
            p.dma(odram[ob * 128:(ob + 1) * 128, :], outbf[:])

    p.finalize(st)
    st.close()
    return nc


def run_l1(inp):
    nc = build_l1()
    ident = np.eye(128, dtype=np.float32)
    im = []
    for c in range(NC):
        b, hg = c // 2, c % 2
        cs = slice(hg * 1024, hg * 1024 + 1024)
        hsl = slice(hg * 8, hg * 8 + 8)
        m = {
            "x": np.ascontiguousarray(inp["L0_attn_x"][b]),
            "wq": np.ascontiguousarray(inp["L0_attn_Wq"][:, cs]),
            "wk": np.ascontiguousarray(inp["L0_attn_Wk"][:, cs]),
            "wv": np.ascontiguousarray(inp["L0_attn_Wv"][:, cs]),
            "wg": np.ascontiguousarray(inp["L0_attn_Wg"][:, cs]),
            "wa": np.ascontiguousarray(inp["L0_attn_Wa"][:, hsl]),
            "wbt": np.ascontiguousarray(inp["L0_attn_Wb"][:, hsl]),
            "bab": np.stack([inp["L0_attn_ba"][hsl], inp["L0_attn_bb"][hsl]]).reshape(2, 8, 1).astype(np.float32),
            "cw": np.stack([inp["L0_attn_qconv_w"][cs, 0, :].reshape(8, 128, 4),
                            inp["L0_attn_kconv_w"][cs, 0, :].reshape(8, 128, 4),
                            inp["L0_attn_vconv_w"][cs, 0, :].reshape(8, 128, 4)]),
            "cb": np.stack([inp["L0_attn_qconv_b"][cs].reshape(8, 128, 1),
                            inp["L0_attn_kconv_b"][cs].reshape(8, 128, 1),
                            inp["L0_attn_vconv_b"][cs].reshape(8, 128, 1)]),
            "ident": ident,
        }
        im.append({k: np.ascontiguousarray(v).astype(np.float32) for k, v in m.items()})
    res = run_bass_kernel_spmd(nc, im, core_ids=list(range(NC)))
    return res.results


def build_l2():
    nc = bass.Bass("TRN2", target_bir_lowering=False)
    kT4 = nc.dram_tensor("kT4", [512, L], F32, kind="ExternalInput").ap()
    qT4 = nc.dram_tensor("qT4", [512, L], F32, kind="ExternalInput").ap()
    v4 = nc.dram_tensor("v4", [4, L, 128], F32, kind="ExternalInput").ap()
    g4 = nc.dram_tensor("g4", [4, L, 128], F32, kind="ExternalInput").ap()
    a4 = nc.dram_tensor("a4", [4, 128, NCH], F32, kind="ExternalInput").ap()
    b4 = nc.dram_tensor("b4", [4, 128, NCH], F32, kind="ExternalInput").ap()
    lngt = nc.dram_tensor("lngt", [128, 128], F32, kind="ExternalInput").ap()
    lnbt = nc.dram_tensor("lnbt", [128, 128], F32, kind="ExternalInput").ap()
    ident = nc.dram_tensor("ident", [128, 128], F32, kind="ExternalInput").ap()
    identb = nc.dram_tensor("identb", [128, 128], BF16, kind="ExternalInput").ap()
    ones1 = nc.dram_tensor("ones1", [1, 128], F32, kind="ExternalInput").ap()
    utri = nc.dram_tensor("utri", [128, 128], F32, kind="ExternalInput").ap()
    nmstrict = nc.dram_tensor("nmstrict", [128, 128], F32, kind="ExternalInput").ap()
    eye128 = nc.dram_tensor("eye128", [128, 128], F32, kind="ExternalInput").ap()
    og4 = nc.dram_tensor("og4", [4, L, 128], F32, kind="ExternalOutput").ap()

    st = ExitStack()
    T = lambda n, s, d=F32: st.enter_context(nc.sbuf_tensor(n, s, d))
    KT = T("KT", [128, 128]); QT = T("QT", [128, 128])
    Vt = T("Vt", [128, 128]); Gt = T("Gt", [128, 128])
    a_all = T("a_all", [128, NCH]); b_all = T("b_all", [128, NCH])
    la_full = T("la_full", [128, NCH]); la2 = T("la2", [128, NCH])
    clf = T("clf", [128, NCH])
    idsf = T("idsf", [128, 128]); idsb = T("idsb", [128, 128], BF16)
    ones_s = T("ones_s", [1, 128]); utri_s = T("utri_s", [128, 128])
    nms_s = T("nms_s", [128, 128]); eye_s = T("eye_s", [128, 128])
    lng_s = T("lng_s", [128, 128]); lnb_s = T("lnb_s", [128, 128])
    clrow = T("clrow", [1, 128]); brow = T("brow", [1, 128]); larow = T("larow", [1, 128])
    negcl = T("negcl", [128, 1]); Rs = T("Rs", [128, 128])
    E2T = T("E2T", [128, 128]); Rla = T("Rla", [128, 128]); E1T = T("E1T", [128, 128])
    Eexp = T("Eexp", [128, 128]); Qs = T("Qs", [128, 128])
    MB = T("MB", [128, 128]); n1 = T("n1", [128, 128]); NT = T("NT", [128, 128])
    m1 = T("m1", [128, 128]); MT = T("MT", [128, 128])
    TTa = T("TTa", [128, 128]); TTb = T("TTb", [128, 128]); TT_bf = T("TT_bf", [128, 128])
    Pa = T("Pa", [128, 128]); Pb = T("Pb", [128, 128])
    PTa = T("PTa", [128, 128]); PTb = T("PTb", [128, 128])
    clprev = T("clprev", [128, 1]); eprev = T("eprev", [128, 1]); coefneg = T("coefneg", [128, 1])
    r1 = T("r1", [128, 128]); RHS = T("RHS", [128, 128])
    U_bf = T("U_bf", [128, 128])
    Kc_bf = T("Kc_bf", [128, 128]); Ke_bf = T("Ke_bf", [128, 128])
    clCs = T("clCs", [1, 1]); clCb = T("clCb", [128, 1]); ecol = T("ecol", [128, 1]); eC = T("eC", [128, 1])
    STa = T("STa", [128, 128]); STb = T("STb", [128, 128])
    osum = T("osum", [128, 1]); omu = T("omu", [128, 1]); oc = T("oc", [128, 128])
    osq = T("osq", [128, 128]); ovs = T("ovs", [128, 1]); osd = T("osd", [128, 1]); ors = T("ors", [128, 1])
    on_ = T("on_", [128, 128]); og1 = T("og1", [128, 128]); og2 = T("og2", [128, 128]); og3 = T("og3", [128, 128])
    arg1 = T("arg1", [128, 128]); arg2 = T("arg2", [128, 128])
    psA = st.enter_context(nc.psum_tensor("psA", [128, 512], F32))
    psB = st.enter_context(nc.psum_tensor("psB", [128, 512], F32))
    psC = st.enter_context(nc.psum_tensor("psC", [128, 512], F32))
    psT = st.enter_context(nc.psum_tensor("psT", [128, 128], F32))

    p = Prog(nc)
    for dst, src in [(idsf, ident), (idsb, identb), (utri_s, utri), (nms_s, nmstrict),
                     (eye_s, eye128), (lng_s, lngt), (lnb_s, lnbt)]:
        p.dma(dst[:], src[:])
    p.dma(ones_s[:], ones1[:])

    for h in range(4):
        p.dma(a_all[:], a4[h]); p.dma(b_all[:], b4[h])
        p.op("scalar", "activation", out=la2[:], in_=a_all[:], func=AF.Ln)
        p.op("vector", "tensor_scalar_max", out=la_full[:], in0=la2[:], scalar1=-60.0)
        p.op("tensor", "matmul", out=psC[:, 256:256 + NCH], lhsT=utri_s[:], rhs=la_full[:], start=True, stop=True)
        p.op("vector", "tensor_copy", out=clf[:], in_=psC[:, 256:256 + NCH])
        p.op("vector", "memset", STa[:], 0.0)
        for c in range(NCH):
            ST_in = STa if c % 2 == 0 else STb
            ST_out = STb if c % 2 == 0 else STa
            cl = clf[:, c:c + 1]
            la = la_full[:, c:c + 1]
            bcol = b_all[:, c:c + 1]
            p.dma(KT[:], kT4[h * 128:(h + 1) * 128, c * 128:(c + 1) * 128])
            p.dma(QT[:], qT4[h * 128:(h + 1) * 128, c * 128:(c + 1) * 128])
            p.dma(Vt[:], v4[h, c * 128:(c + 1) * 128, :])
            p.dma(Gt[:], g4[h, c * 128:(c + 1) * 128, :])
            p.op("vector", "tensor_scalar_mul", out=negcl[:], in0=cl, scalar1=-1.0)
            p.op("tensor", "transpose", out=psA[0:1, 0:128], in_=cl, identity=idsf[:])
            p.op("scalar", "copy", out=clrow[:], in_=psA[0:1, 0:128])
            p.op("tensor", "transpose", out=psA[0:1, 128:256], in_=bcol, identity=idsf[:])
            p.op("scalar", "copy", out=brow[:], in_=psA[0:1, 128:256])
            p.op("tensor", "transpose", out=psA[0:1, 256:384], in_=la, identity=idsf[:])
            p.op("scalar", "copy", out=larow[:], in_=psA[0:1, 256:384])
            p.op("tensor", "matmul", out=psA[:, 0:128], lhsT=ones_s[:], rhs=clrow[:], start=True, stop=True)
            p.op("vector", "tensor_copy", out=Rs[:], in_=psA[:, 0:128])
            p.op("tensor", "matmul", out=psA[:, 128:256], lhsT=ones_s[:], rhs=larow[:], start=True, stop=True)
            p.op("tensor", "matmul", out=psA[:, 256:384], lhsT=ones_s[:], rhs=brow[:], start=True, stop=True)
            p.op("vector", "tensor_scalar", out=arg1[:], in0=Rs[:], scalar1=negcl[:], scalar2=0.0,
                 op0=ALU.add, op1=ALU.min)
            p.op("scalar", "activation", out=E2T[:], in_=arg1[:], func=AF.Exp)
            p.op("vector", "tensor_sub", out=Rla[:], in0=Rs[:], in1=psA[:, 128:256])
            p.op("vector", "tensor_scalar", out=arg2[:], in0=Rla[:], scalar1=negcl[:], scalar2=0.0,
                 op0=ALU.add, op1=ALU.min)
            p.op("scalar", "activation", out=E1T[:], in_=arg2[:], func=AF.Exp)
            p.op("scalar", "activation", out=Eexp[:], in_=Rs[:], func=AF.Exp)
            p.op("vector", "tensor_mul", out=Qs[:], in0=QT[:], in1=Eexp[:])
            p.op("vector", "tensor_mul", out=MB[:], in0=psA[:, 256:384], in1=nms_s[:])
            p.op("tensor", "matmul", out=psB[:, 0:128], lhsT=KT[:], rhs=KT[:], start=True, stop=True)
            p.op("tensor", "matmul", out=psB[:, 128:256], lhsT=KT[:], rhs=QT[:], start=True, stop=True)
            p.op("vector", "tensor_mul", out=n1[:], in0=E1T[:], in1=psB[:, 0:128])
            p.op("vector", "tensor_mul", out=NT[:], in0=n1[:], in1=MB[:])
            p.op("vector", "tensor_mul", out=m1[:], in0=E2T[:], in1=psB[:, 128:256])
            p.op("vector", "tensor_mul", out=MT[:], in0=m1[:], in1=utri_s[:])
            p.op("vector", "tensor_add", out=TTa[:], in0=NT[:], in1=eye_s[:])
            p.op("scalar", "copy", out=TT_bf[:], in_=TTa[:])
            p.op("tensor", "transpose", out=psT[:], in_=NT[:], identity=idsf[:])
            p.op("scalar", "copy", out=Pa[:], in_=psT[:])
            p.op("scalar", "copy", out=PTa[:], in_=NT[:])
            TT_cur, TT_oth = TTa, TTb
            P_cur, PT_cur, P_oth, PT_oth = Pa, PTa, Pb, PTb
            for lev in range(6):
                p.op("tensor", "matmul", out=psC[:, 256:384], lhsT=PT_cur[:], rhs=P_cur[:], start=True, stop=True)
                p.op("scalar", "copy", out=P_oth[:], in_=psC[:, 256:384])
                p.op("tensor", "transpose", out=psT[:], in_=P_oth[:], identity=idsf[:])
                p.op("scalar", "copy", out=PT_oth[:], in_=psT[:])
                p.op("tensor", "matmul", out=psC[:, 384:512], lhsT=P_oth[:], rhs=TT_bf[:], start=True, stop=True)
                p.op("vector", "tensor_add", out=TT_oth[:], in0=TT_cur[:], in1=psC[:, 384:512])
                p.op("scalar", "copy", out=TT_bf[:], in_=TT_oth[:])
                TT_cur, TT_oth = TT_oth, TT_cur
                P_cur, PT_cur, P_oth, PT_oth = P_oth, PT_oth, P_cur, PT_cur
            p.op("vector", "tensor_sub", out=clprev[:], in0=cl, in1=la)
            p.op("scalar", "activation", out=eprev[:], in_=clprev[:], func=AF.Exp)
            p.op("vector", "tensor_scalar", out=coefneg[:], in0=eprev[:], scalar1=bcol, scalar2=-1.0,
                 op0=ALU.mult, op1=ALU.mult)
            p.op("vector", "tensor_scalar_mul", out=r1[:], in0=Vt[:], scalar1=bcol)
            p.op("tensor", "matmul", out=psB[:, 256:384], lhsT=KT[:], rhs=ST_in[:], start=True, stop=True)
            p.op("vector", "scalar_tensor_tensor", out=RHS[:], in0=psB[:, 256:384], scalar=coefneg[:],
                 in1=r1[:], op0=ALU.mult, op1=ALU.add)
            p.op("tensor", "matmul", out=psB[:, 384:512], lhsT=TT_bf[:], rhs=RHS[:], start=True, stop=True)
            p.op("scalar", "copy", out=U_bf[:], in_=psB[:, 384:512])
            p.op("tensor", "matmul", out=psC[:, 0:128], lhsT=Qs[:], rhs=ST_in[:], start=True, stop=False)
            p.op("tensor", "matmul", out=psC[:, 0:128], lhsT=MT[:], rhs=U_bf[:], start=False, stop=True)
            p.op("tensor", "transpose", out=psT[:], in_=KT[:], identity=idsf[:])
            p.op("scalar", "copy", out=Kc_bf[:], in_=psT[:])
            p.op("scalar", "copy", out=clCs[:], in_=clrow[0:1, 127:128])
            p.op("tensor", "matmul", out=psA[:, 384:385], lhsT=ones_s[:], rhs=clCs[:], start=True, stop=True)
            p.op("vector", "tensor_copy", out=clCb[:], in_=psA[:, 384:385])
            p.op("vector", "tensor_sub", out=eprev[:], in0=clCb[:], in1=cl)
            p.op("scalar", "activation", out=ecol[:], in_=eprev[:], func=AF.Exp)
            p.op("scalar", "activation", out=eC[:], in_=clCb[:], func=AF.Exp)
            p.op("vector", "tensor_scalar_mul", out=Ke_bf[:], in0=Kc_bf[:], scalar1=ecol[:])
            p.op("tensor", "matmul", out=psC[:, 128:256], lhsT=Ke_bf[:], rhs=U_bf[:], start=True, stop=True)
            p.op("vector", "scalar_tensor_tensor", out=ST_out[:], in0=ST_in[:], scalar=eC[:],
                 in1=psC[:, 128:256], op0=ALU.mult, op1=ALU.add)
            p.op("vector", "reduce_sum", out=osum[:], in_=psC[:, 0:128], axis=AX.X)
            p.op("vector", "tensor_scalar_mul", out=omu[:], in0=osum[:], scalar1=1.0 / 128)
            p.op("vector", "tensor_scalar_sub", out=oc[:], in0=psC[:, 0:128], scalar1=omu[:])
            p.op("scalar", "activation", out=osq[:], in_=oc[:], func=AF.Square)
            p.op("vector", "reduce_sum", out=ovs[:], in_=osq[:], axis=AX.X)
            p.op("vector", "tensor_scalar", out=ovs[:], in0=ovs[:], scalar1=1.0 / 128, scalar2=1e-5,
                 op0=ALU.mult, op1=ALU.add)
            p.op("scalar", "activation", out=osd[:], in_=ovs[:], func=AF.Sqrt)
            p.op("vector", "reciprocal", out=ors[:], in_=osd[:])
            p.op("vector", "tensor_scalar_mul", out=on_[:], in0=oc[:], scalar1=ors[:])
            p.op("vector", "tensor_mul", out=og1[:], in0=on_[:], in1=lng_s[:])
            p.op("vector", "tensor_add", out=og2[:], in0=og1[:], in1=lnb_s[:])
            p.op("vector", "tensor_mul", out=og3[:], in0=og2[:], in1=Gt[:])
            p.dma(og4[h, c * 128:(c + 1) * 128, :], og3[:])

    p.finalize(st)
    st.close()
    return nc


def run_l2(r1res):
    nc = build_l2()
    ident = np.eye(128, dtype=np.float32)
    identb = ident.astype(BF)
    ones1 = np.ones((1, 128), np.float32)
    utri = np.triu(np.ones((128, 128), np.float32))
    nmstrict = -np.triu(np.ones((128, 128), np.float32), 1)
    og = [[None, None] for _ in range(NC)]
    for hh in range(2):
        im = []
        for c in range(NC):
            r = r1res[c]
            rs = slice(hh * 512, hh * 512 + 512)
            v4 = np.ascontiguousarray(
                np.asarray(r["vcT"][rs], np.float32).reshape(4, 128, L).transpose(0, 2, 1))
            g4 = np.ascontiguousarray(
                np.asarray(r["gT"][rs], np.float32).reshape(4, 128, L).transpose(0, 2, 1))
            a4 = np.ascontiguousarray(
                r["ab"][0][hh * 4:hh * 4 + 4].reshape(4, NCH, 128).transpose(0, 2, 1))
            b4 = np.ascontiguousarray(
                r["ab"][1][hh * 4:hh * 4 + 4].reshape(4, NCH, 128).transpose(0, 2, 1))
            im.append({
                "kT4": np.ascontiguousarray(np.asarray(r["kcT"], np.float32)[rs]),
                "qT4": np.ascontiguousarray(np.asarray(r["qcT"], np.float32)[rs]),
                "v4": v4, "g4": g4,
                "a4": a4.astype(np.float32), "b4": b4.astype(np.float32),
                "lngt": np.broadcast_to(LNG, (128, 128)).astype(np.float32).copy(),
                "lnbt": np.broadcast_to(LNB, (128, 128)).astype(np.float32).copy(),
                "ident": ident, "identb": identb, "ones1": ones1,
                "utri": utri, "nmstrict": nmstrict, "eye128": ident,
            })
        res = run_bass_kernel_spmd(nc, im, core_ids=list(range(NC)))
        for c in range(NC):
            og[c][hh] = np.asarray(res.results[c]["og4"])
    return og


def build_l3():
    nc = bass.Bass("TRN2", target_bir_lowering=False)
    oT = nc.dram_tensor("oT", [D, 1024], F32, kind="ExternalInput").ap()
    wo = nc.dram_tensor("wo", [D, D], F32, kind="ExternalInput").ap()
    xns = nc.dram_tensor("xns", [1024, D], F32, kind="ExternalInput").ap()
    gam = nc.dram_tensor("gam", [128, D], F32, kind="ExternalInput").ap()
    bet = nc.dram_tensor("bet", [128, D], F32, kind="ExternalInput").ap()
    slo = nc.dram_tensor("slo", [64, 512], F32, kind="ExternalInput").ap()
    ident = nc.dram_tensor("ident", [128, 128], F32, kind="ExternalInput").ap()
    io8 = nc.dram_tensor("io8", [128, 8], F32, kind="ExternalInput").ap()
    io64 = nc.dram_tensor("io64", [128, 64], F32, kind="ExternalInput").ap()
    ctb = nc.dram_tensor("ctb", [128, 64], F32, kind="ExternalInput").ap()
    ao = nc.dram_tensor("ao", [1024, D], F32, kind="ExternalOutput").ap()
    ti_o = nc.dram_tensor("ti_o", [128, 8], F32, kind="ExternalOutput").ap()
    xsl_o = nc.dram_tensor("xsl_o", [128, 256], F32, kind="ExternalOutput").ap()
    pmin_o = nc.dram_tensor("pmin_o", [64, 1], F32, kind="ExternalOutput").ap()
    pmax_o = nc.dram_tensor("pmax_o", [64, 1], F32, kind="ExternalOutput").ap()

    st = ExitStack()
    T = lambda n, s, d=F32: st.enter_context(nc.sbuf_tensor(n, s, d))
    oT_bf = T("oT_bf", [128, 16 * 1024], BF16)
    wo_bf = T("wo_bf", [128, 16 * D], BF16)
    ldr = T("ldr", [128, 2048])
    ao_sb = T("ao_sb", [128, D])
    ids = T("ids", [128, 128])
    sig_bf = T("sig_bf", [128, 16 * 1024], BF16)
    tsr = T("tsr", [128, 16 * 72], BF16)
    ssA = T("ssA", [64, 32]); ssB = T("ssB", [64, 32])
    ssg = T("ssg", [64, 32]); ssgb = T("ssgb", [64, 32], BF16)
    ssT = T("ssT", [32, 64], BF16)
    slo_sb = T("slo_sb", [64, 512])
    xt = T("xt", [128, D]); x2 = T("x2", [128, D]); sq = T("sq", [128, D])
    stok = T("stok", [128, D])
    scol = T("scol", [128, 1]); mucol = T("mucol", [128, 1]); vcol = T("vcol", [128, 1])
    sdcol = T("sdcol", [128, 1]); rscol = T("rscol", [128, 1])
    sc_sb = T("sc_sb", [128, 72])
    mx = T("mx", [128, 1]); eq8 = T("eq8", [128, 8]); t8 = T("t8", [128, 8])
    bc = T("bc", [128, 1]); ic64 = T("ic64", [128, 64]); ms64 = T("ms64", [128, 64])
    eq64 = T("eq64", [128, 64]); t64 = T("t64", [128, 64])
    io8s = T("io8s", [128, 8]); io64s = T("io64s", [128, 64]); ctbs = T("ctbs", [128, 64])
    ti_col = T("ti_col", [128, 8])
    mkcol = T("mkcol", [128, 1])
    xslA = T("xslA", [128, 32]); xslB = T("xslB", [128, 32])
    xsl_sb = T("xsl_sb", [128, 256])
    tokmin = T("tokmin", [128, 8]); tokmax = T("tokmax", [128, 8])
    mk8 = T("mk8", [128, 8]); sel8 = T("sel8", [128, 8])
    sel8b = T("sel8b", [128, 8]); sel8c = T("sel8c", [128, 8])
    inf8 = T("inf8", [128, 8]); ninf8 = T("ninf8", [128, 8])
    stmin = T("stmin", [128, 64]); stmax = T("stmax", [128, 64])
    sttr = T("sttr", [64, 128])
    idsb3 = T("idsb3", [128, 128], BF16)
    pmin_s = T("pmin_s", [64, 1]); pmax_s = T("pmax_s", [64, 1])
    ps_mm = st.enter_context(nc.psum_tensor("ps_mm", [128, 512], F32))
    ps_tr = st.enter_context(nc.psum_tensor("ps_tr", [128, 128], F32))
    ps_trb = st.enter_context(nc.psum_tensor("ps_trb", [128, 128], BF16))

    p = Prog(nc)
    p.dma(ids[:], ident[:])
    p.dma(io8s[:], io8[:]); p.dma(io64s[:], io64[:]); p.dma(ctbs[:], ctb[:])
    p.dma(slo_sb[:], slo[:])
    p.op("vector", "memset", inf8[:], INF)
    p.op("vector", "memset", ninf8[:], -INF)
    p.op("scalar", "copy", out=idsb3[:], in_=ids[:])

    for blk in range(16):
        p.dma(ldr[:, 0:1024], oT[blk * 128:(blk + 1) * 128, :])
        p.op("scalar", "copy", out=oT_bf[:, blk * 1024:(blk + 1) * 1024], in_=ldr[:, 0:1024])
    for oc in range(16):
        p.dma(ldr[:], wo[oc * 128:(oc + 1) * 128, :])
        p.op("scalar", "copy", out=wo_bf[:, oc * D:(oc + 1) * D], in_=ldr[:])
    for nb in range(8):
        for ch in range(4):
            for oc in range(16):
                p.op("tensor", "matmul", out=ps_mm[:, 0:512],
                     lhsT=oT_bf[:, oc * 1024 + nb * 128: oc * 1024 + nb * 128 + 128],
                     rhs=wo_bf[:, oc * D + ch * 512: oc * D + ch * 512 + 512],
                     start=(oc == 0), stop=(oc == 15))
            p.op("scalar", "copy", out=ao_sb[:, ch * 512:(ch + 1) * 512], in_=ps_mm[:, 0:512])
        p.dma(ao[nb * 128:(nb + 1) * 128, :], ao_sb[:])

    view = slo_sb[:].rearrange("p (j g) -> p g j", g=16)
    p.op("vector", "tensor_copy", out=ssA[:], in_=view[:, 0, :])
    cur, oth = ssA, ssB
    for g in range(1, 16):
        p.op("vector", "tensor_add", out=oth[:], in0=cur[:], in1=view[:, g, :])
        cur, oth = oth, cur
    p.op("scalar", "activation", out=ssg[:], in_=cur[:], func=AF.Sign)
    p.op("scalar", "copy", out=ssgb[:], in_=ssg[:])
    p.op("tensor", "transpose", out=ps_trb[0:32, 0:64], in_=ssgb[0:64, 0:32], identity=idsb3[0:64, 0:64])
    p.op("scalar", "copy", out=ssT[:], in_=ps_trb[0:32, 0:64])
    p.op("vector", "memset", tsr[:], 0.0)
    for db in range(16):
        for cc in range(4):
            i = 4 * db + cc
            p.dma(tsr[32 * cc:32 * cc + 32, db * 72 + i: db * 72 + i + 1], ssT[0:32, i:i + 1])
            p.dma(tsr[32 * cc:32 * cc + 32, db * 72 + 64 + db // 2: db * 72 + 64 + db // 2 + 1], ssT[0:32, i:i + 1])

    for nb in range(8):
        p.dma(xt[:], xns[nb * 128:(nb + 1) * 128, :])
        _emit_plain_ln(p, xt[:], x2[:], sq[:], scol[:], mucol[:], vcol[:], sdcol[:], rscol[:], D)
        p.dma(ldr[:], gam[:])
        p.op("vector", "tensor_mul", out=x2[:], in0=x2[:], in1=ldr[:])
        p.dma(ldr[:], bet[:])
        p.op("vector", "tensor_add", out=x2[:], in0=x2[:], in1=ldr[:])
        p.op("scalar", "activation", out=stok[:], in_=x2[:], func=AF.Sign)
        for db in range(16):
            p.op("tensor", "transpose", out=ps_tr[:], in_=stok[:, db * 128:(db + 1) * 128], identity=ids[:])
            p.op("scalar", "copy", out=sig_bf[:, db * 1024 + nb * 128: db * 1024 + nb * 128 + 128], in_=ps_tr[:])

    for nb in range(8):
        for db in range(16):
            p.op("tensor", "matmul", out=ps_mm[:, 0:72],
                 lhsT=sig_bf[:, db * 1024 + nb * 128: db * 1024 + nb * 128 + 128],
                 rhs=tsr[:, db * 72:(db + 1) * 72],
                 start=(db == 0), stop=(db == 15))
        p.op("vector", "tensor_copy", out=sc_sb[:], in_=ps_mm[:, 0:72])
        cs = sc_sb[:, 64:72]
        p.op("vector", "reduce_max", out=mx[:], in_=cs, axis=AX.X)
        p.op("vector", "tensor_scalar", out=eq8[:], in0=cs, scalar1=mx[:], scalar2=None, op0=ALU.is_equal)
        p.op("vector", "tensor_scalar", out=t8[:], in0=eq8[:], scalar1=-BIG, scalar2=BIG, op0=ALU.mult, op1=ALU.add)
        p.op("vector", "tensor_add", out=t8[:], in0=t8[:], in1=io8s[:])
        p.op("vector", "tensor_reduce", out=bc[:], in_=t8[:], axis=AX.X, op=ALU.min)
        p.op("vector", "tensor_scalar", out=ic64[:], in0=ctbs[:], scalar1=bc[:], scalar2=None, op0=ALU.is_equal)
        p.op("vector", "scalar_tensor_tensor", out=ms64[:], in0=ic64[:], scalar=BIG, in1=sc_sb[:, 0:64],
             op0=ALU.mult, op1=ALU.add)
        p.op("vector", "reduce_max", out=mx[:], in_=ms64[:], axis=AX.X)
        p.op("vector", "tensor_scalar", out=eq64[:], in0=ms64[:], scalar1=mx[:], scalar2=None, op0=ALU.is_equal)
        p.op("vector", "tensor_scalar", out=t64[:], in0=eq64[:], scalar1=-BIG, scalar2=BIG, op0=ALU.mult, op1=ALU.add)
        p.op("vector", "tensor_add", out=t64[:], in0=t64[:], in1=io64s[:])
        p.op("vector", "tensor_reduce", out=ti_col[:, nb:nb + 1], in_=t64[:], axis=AX.X, op=ALU.min)

    for nb in range(8):
        p.dma(xt[:], xns[nb * 128:(nb + 1) * 128, :])
        _emit_plain_ln(p, xt[:], x2[:], sq[:], scol[:], mucol[:], vcol[:], sdcol[:], rscol[:], D)
        p.dma(ldr[:], gam[:])
        p.op("vector", "tensor_mul", out=x2[:], in0=x2[:], in1=ldr[:])
        p.dma(ldr[:], bet[:])
        p.op("vector", "tensor_add", out=x2[:], in0=x2[:], in1=ldr[:])
        p.op("vector", "memset", xslA[:], 0.0)
        cur, oth = xslA, xslB
        for i in range(64):
            p.op("vector", "tensor_single_scalar", out=mkcol[:], in_=ti_col[:, nb:nb + 1],
                 scalar=float(i), op=ALU.is_equal)
            p.op("vector", "scalar_tensor_tensor", out=oth[:], in0=x2[:, 32 * i:32 * i + 32],
                 scalar=mkcol[:], in1=cur[:], op0=ALU.mult, op1=ALU.add)
            cur, oth = oth, cur
        p.op("vector", "tensor_copy", out=xsl_sb[:, nb * 32:(nb + 1) * 32], in_=cur[:])
        p.op("vector", "tensor_reduce", out=tokmin[:, nb:nb + 1], in_=cur[:], axis=AX.X, op=ALU.min)
        p.op("vector", "reduce_max", out=tokmax[:, nb:nb + 1], in_=cur[:], axis=AX.X)
    p.dma(ti_o[:], ti_col[:])
    p.dma(xsl_o[:], xsl_sb[:])

    for i in range(64):
        p.op("vector", "tensor_single_scalar", out=mk8[:], in_=ti_col[:], scalar=float(i), op=ALU.is_equal)
        p.op("vector", "tensor_mul", out=sel8[:], in0=tokmin[:], in1=mk8[:])
        p.op("vector", "tensor_scalar", out=sel8b[:], in0=mk8[:], scalar1=-INF, scalar2=INF,
             op0=ALU.mult, op1=ALU.add)
        p.op("vector", "tensor_add", out=sel8c[:], in0=sel8[:], in1=sel8b[:])
        p.op("vector", "tensor_reduce", out=stmin[:, i:i + 1], in_=sel8c[:], axis=AX.X, op=ALU.min)
        p.op("vector", "tensor_mul", out=sel8[:], in0=tokmax[:], in1=mk8[:])
        p.op("vector", "tensor_scalar", out=sel8b[:], in0=mk8[:], scalar1=INF, scalar2=-INF,
             op0=ALU.mult, op1=ALU.add)
        p.op("vector", "tensor_add", out=sel8c[:], in0=sel8[:], in1=sel8b[:])
        p.op("vector", "reduce_max", out=stmax[:, i:i + 1], in_=sel8c[:], axis=AX.X)
    p.op("tensor", "transpose", out=ps_tr[0:64, 0:128], in_=stmin[:], identity=ids[:])
    p.op("scalar", "copy", out=sttr[:], in_=ps_tr[0:64, 0:128])
    p.op("vector", "tensor_reduce", out=pmin_s[:], in_=sttr[:], axis=AX.X, op=ALU.min)
    p.dma(pmin_o[:], pmin_s[:])
    p.op("tensor", "transpose", out=ps_tr[0:64, 0:128], in_=stmax[:], identity=ids[:])
    p.op("scalar", "copy", out=sttr[:], in_=ps_tr[0:64, 0:128])
    p.op("vector", "reduce_max", out=pmax_s[:], in_=sttr[:], axis=AX.X)
    p.dma(pmax_o[:], pmax_s[:])

    p.finalize(st)
    st.close()
    return nc


def run_l3(inp, oT_full, xn_full):
    nc = build_l3()
    ident = np.eye(128, dtype=np.float32)
    io8 = np.broadcast_to(np.arange(8, dtype=np.float32), (128, 8)).copy()
    io64 = np.broadcast_to(np.arange(64, dtype=np.float32), (128, 64)).copy()
    ctb = np.broadcast_to((np.arange(64) // 8).astype(np.float32), (128, 64)).copy()
    slo = inp["L0_ffn_spline_slopes"].reshape(64, 512).astype(np.float32)
    gam = np.broadcast_to(inp["L0_ffn_ln_gamma"], (128, D)).astype(np.float32).copy()
    bet = np.broadcast_to(inp["L0_ffn_ln_beta"], (128, D)).astype(np.float32).copy()
    im = []
    for c in range(NC):
        b, half = c // 2, c % 2
        trange = slice(half * 1024, half * 1024 + 1024)
        im.append({
            "oT": np.ascontiguousarray(oT_full[b][:, trange]),
            "wo": inp["L0_attn_Wo"].astype(np.float32),
            "xns": np.ascontiguousarray(xn_full[b][trange]),
            "gam": gam, "bet": bet, "slo": slo, "ident": ident,
            "io8": io8, "io64": io64, "ctb": ctb,
        })
    res = run_bass_kernel_spmd(nc, im, core_ids=list(range(NC)))
    return res.results


def build_l4():
    nc = bass.Bass("TRN2", target_bir_lowering=False)
    pmin = nc.dram_tensor("pmin", [64, 8], F32, kind="ExternalInput").ap()
    pmax = nc.dram_tensor("pmax", [64, 8], F32, kind="ExternalInput").ap()
    ti_i = nc.dram_tensor("ti_i", [128, 8], F32, kind="ExternalInput").ap()
    tir = nc.dram_tensor("tir", [1, 1024], F32, kind="ExternalInput").ap()
    xsl_i = nc.dram_tensor("xsl_i", [128, 256], F32, kind="ExternalInput").ap()
    bas = nc.dram_tensor("bas", [64, 512], F32, kind="ExternalInput").ap()
    slo = nc.dram_tensor("slo", [64, 512], F32, kind="ExternalInput").ap()
    osc = nc.dram_tensor("osc", [64, 1], F32, kind="ExternalInput").ap()
    io64c = nc.dram_tensor("io64c", [64, 1], F32, kind="ExternalInput").ap()
    ones1 = nc.dram_tensor("ones1", [1, 64], F32, kind="ExternalInput").ap()
    xs_d = nc.dram_tensor("xs_d", [1024, D], F32, kind="ExternalInput").ap()
    xns_d = nc.dram_tensor("xns_d", [1024, D], F32, kind="ExternalInput").ap()
    ao_d = nc.dram_tensor("ao_d", [1024, D], F32, kind="ExternalInput").ap()
    y0 = nc.dram_tensor("y0", [1024, D], F32, kind="ExternalOutput").ap()

    st = ExitStack()
    T = lambda n, s, d=F32: st.enter_context(nc.sbuf_tensor(n, s, d))
    pmin_s = T("pmin_s", [64, 8]); pmax_s = T("pmax_s", [64, 8])
    tmin = T("tmin", [64, 1]); tmax = T("tmax", [64, 1])
    rhs3 = T("rhs3", [64, 3])
    tir_s = T("tir_s", [1, 1024])
    oh = T("oh", [64, 1024])
    io64s = T("io64s", [64, 1]); ones_s = T("ones_s", [1, 64])
    bas_s = T("bas_s", [64, 512]); slo_s = T("slo_s", [64, 512]); osc_s = T("osc_s", [64, 1])
    ti_col = T("ti_col", [128, 8]); xsl = T("xsl", [128, 256])
    g3 = T("g3", [128, 3])
    btok = T("btok", [128, 512]); stok = T("stok", [128, 512])
    den = T("den", [128, 1]); rden = T("rden", [128, 1])
    u0 = T("u0", [128, 32]); u1 = T("u1", [128, 32]); uu = T("uu", [128, 32])
    u16 = T("u16", [128, 32]); gi = T("gi", [128, 32]); gif = T("gif", [128, 32])
    iti = st.enter_context(nc.sbuf_tensor("iti", [128, 32], I32))
    xl = T("xl", [128, 32])
    mk32 = T("mk32", [128, 32]); tsel = T("tsel", [128, 32])
    bselA = T("bselA", [128, 32]); bselB = T("bselB", [128, 32])
    sselA = T("sselA", [128, 32]); sselB = T("sselB", [128, 32])
    d0 = T("d0", [128, 32]); delta = T("delta", [128, 32])
    xb = T("xb", [128, D]); xnb = T("xnb", [128, D]); aob = T("aob", [128, D])
    s2 = T("s2", [128, D]); y0b = T("y0b", [128, D])
    mkcol = T("mkcol", [128, 1]); ctile = T("ctile", [128, 32])
    ps_mm = st.enter_context(nc.psum_tensor("ps_mm", [128, 512], F32))
    ps_b = st.enter_context(nc.psum_tensor("ps_b", [128, 512], F32))

    p = Prog(nc)
    p.dma(pmin_s[:], pmin[:]); p.dma(pmax_s[:], pmax[:])
    p.dma(tir_s[:], tir[:]); p.dma(io64s[:], io64c[:]); p.dma(ones_s[:], ones1[:])
    p.dma(bas_s[:], bas[:]); p.dma(slo_s[:], slo[:]); p.dma(osc_s[:], osc[:])
    p.dma(ti_col[:], ti_i[:]); p.dma(xsl[:], xsl_i[:])
    p.op("vector", "tensor_reduce", out=tmin[:], in_=pmin_s[:], axis=AX.X, op=ALU.min)
    p.op("vector", "reduce_max", out=tmax[:], in_=pmax_s[:], axis=AX.X)
    p.op("vector", "tensor_copy", out=rhs3[:, 0:1], in_=tmin[:])
    p.op("vector", "tensor_copy", out=rhs3[:, 1:2], in_=tmax[:])
    p.op("vector", "tensor_copy", out=rhs3[:, 2:3], in_=osc_s[:])
    for ch in range(2):
        p.op("tensor", "matmul", out=ps_b[0:64, 0:512], lhsT=ones_s[:], rhs=tir_s[0:1, ch * 512:(ch + 1) * 512],
             start=True, stop=True)
        p.op("vector", "tensor_scalar", out=oh[:, ch * 512:(ch + 1) * 512], in0=ps_b[0:64, 0:512],
             scalar1=io64s[:], scalar2=None, op0=ALU.is_equal)

    for nb in range(8):
        ohs = oh[:, nb * 128:(nb + 1) * 128]
        p.op("tensor", "matmul", out=ps_mm[:, 0:3], lhsT=ohs, rhs=rhs3[:], start=True, stop=True)
        p.op("vector", "tensor_copy", out=g3[:], in_=ps_mm[:, 0:3])
        p.op("tensor", "matmul", out=ps_mm[:, 0:512], lhsT=ohs, rhs=bas_s[:], start=True, stop=True)
        p.op("vector", "tensor_copy", out=btok[:], in_=ps_mm[:, 0:512])
        p.op("tensor", "matmul", out=ps_mm[:, 0:512], lhsT=ohs, rhs=slo_s[:], start=True, stop=True)
        p.op("vector", "tensor_copy", out=stok[:], in_=ps_mm[:, 0:512])
        xs32 = xsl[:, nb * 32:(nb + 1) * 32]
        p.op("vector", "scalar_tensor_tensor", out=den[:], in0=g3[:, 1:2], scalar=1e-8, in1=g3[:, 0:1],
             op0=ALU.add, op1=ALU.subtract)
        p.op("vector", "reciprocal", out=rden[:], in_=den[:])
        p.op("vector", "tensor_scalar_sub", out=u0[:], in0=xs32, scalar1=g3[:, 0:1])
        p.op("vector", "tensor_scalar_mul", out=u1[:], in0=u0[:], scalar1=rden[:])
        p.op("vector", "tensor_scalar", out=uu[:], in0=u1[:], scalar1=0.0, scalar2=1.0 - 1e-6,
             op0=ALU.max, op1=ALU.min)
        p.op("vector", "tensor_scalar_mul", out=u16[:], in0=uu[:], scalar1=16.0)
        p.op("vector", "tensor_single_scalar", out=gif[:], in_=u16[:], scalar=0.5, op=ALU.subtract)
        _round_to_int(p, gif[:], iti[:])
        p.op("vector", "tensor_sub", out=xl[:], in0=u16[:], in1=gif[:])
        p.op("vector", "memset", bselA[:], 0.0)
        p.op("vector", "memset", sselA[:], 0.0)
        bcur, both = bselA, bselB
        scur, soth = sselA, sselB
        bview = btok[:].rearrange("p (j g) -> p g j", g=16)
        sview = stok[:].rearrange("p (j g) -> p g j", g=16)
        for g in range(16):
            p.op("vector", "tensor_single_scalar", out=mk32[:], in_=gif[:], scalar=float(g), op=ALU.is_equal)
            p.op("vector", "tensor_mul", out=tsel[:], in0=mk32[:], in1=bview[:, g, :])
            p.op("vector", "tensor_add", out=both[:], in0=bcur[:], in1=tsel[:])
            bcur, both = both, bcur
            p.op("vector", "tensor_mul", out=tsel[:], in0=mk32[:], in1=sview[:, g, :])
            p.op("vector", "tensor_add", out=soth[:], in0=scur[:], in1=tsel[:])
            scur, soth = soth, scur
        p.op("vector", "tensor_mul", out=d0[:], in0=scur[:], in1=xl[:])
        p.op("vector", "tensor_add", out=d0[:], in0=d0[:], in1=bcur[:])
        p.op("vector", "tensor_scalar_mul", out=delta[:], in0=d0[:], scalar1=g3[:, 2:3])
        p.dma(xb[:], xs_d[nb * 128:(nb + 1) * 128, :])
        p.dma(xnb[:], xns_d[nb * 128:(nb + 1) * 128, :])
        p.dma(aob[:], ao_d[nb * 128:(nb + 1) * 128, :])
        p.op("vector", "tensor_add", out=s2[:], in0=xb[:], in1=aob[:])
        p.op("vector", "tensor_add", out=s2[:], in0=s2[:], in1=xnb[:])
        for i in range(64):
            p.op("vector", "tensor_single_scalar", out=mkcol[:], in_=ti_col[:, nb:nb + 1],
                 scalar=float(i), op=ALU.is_equal)
            p.op("vector", "tensor_scalar_mul", out=ctile[:], in0=delta[:], scalar1=mkcol[:])
            p.op("vector", "tensor_add", out=y0b[:, 32 * i:32 * i + 32], in0=s2[:, 32 * i:32 * i + 32],
                 in1=ctile[:])
        p.dma(y0[nb * 128:(nb + 1) * 128, :], y0b[:])

    p.finalize(st)
    st.close()
    return nc


def run_l4(inp, r3, xn_full, x_flat):
    nc = build_l4()
    pmin = np.stack([np.asarray(r3[c]["pmin_o"])[:, 0] for c in range(NC)], axis=1).astype(np.float32)
    pmax = np.stack([np.asarray(r3[c]["pmax_o"])[:, 0] for c in range(NC)], axis=1).astype(np.float32)
    bas = inp["L0_ffn_spline_bases"].reshape(64, 512).astype(np.float32)
    slo = inp["L0_ffn_spline_slopes"].reshape(64, 512).astype(np.float32)
    osc = inp["L0_ffn_output_scale"].reshape(64, 1).astype(np.float32)
    io64c = np.arange(64, dtype=np.float32).reshape(64, 1)
    ones1 = np.ones((1, 64), np.float32)
    im = []
    for c in range(NC):
        b, half = c // 2, c % 2
        trange = slice(half * 1024, half * 1024 + 1024)
        ti_col = np.asarray(r3[c]["ti_o"])
        tir = np.ascontiguousarray(ti_col.T.reshape(1, 1024))
        im.append({
            "pmin": pmin, "pmax": pmax,
            "ti_i": ti_col.astype(np.float32), "tir": tir.astype(np.float32),
            "xsl_i": np.asarray(r3[c]["xsl_o"]).astype(np.float32),
            "bas": bas, "slo": slo, "osc": osc, "io64c": io64c, "ones1": ones1,
            "xs_d": np.ascontiguousarray(x_flat[c * 1024:(c + 1) * 1024]),
            "xns_d": np.ascontiguousarray(xn_full[b][trange]),
            "ao_d": np.asarray(r3[c]["ao"]).astype(np.float32),
        })
    res = run_bass_kernel_spmd(nc, im, core_ids=list(range(NC)))
    return res.results


def build_l5():
    nc = bass.Bass("TRN2", target_bir_lowering=False)
    y0d = nc.dram_tensor("y0d", [1024, D], F32, kind="ExternalInput").ap()
    wi = nc.dram_tensor("wi", [D, D], F32, kind="ExternalInput").ap()
    wf = nc.dram_tensor("wf", [D, D], F32, kind="ExternalInput").ap()
    wgg = nc.dram_tensor("wgg", [D, D], F32, kind="ExternalInput").ap()
    si = nc.dram_tensor("si", [D, 1], F32, kind="ExternalInput").ap()
    sf = nc.dram_tensor("sf", [D, 1], F32, kind="ExternalInput").ap()
    sg = nc.dram_tensor("sg", [D, 1], F32, kind="ExternalInput").ap()
    ident = nc.dram_tensor("ident", [128, 128], F32, kind="ExternalInput").ap()
    fT = nc.dram_tensor("fT", [D, 1024], F32, kind="ExternalOutput").ap()
    iT = nc.dram_tensor("iT", [D, 1024], F32, kind="ExternalOutput").ap()
    gsT = nc.dram_tensor("gsT", [D, 1024], F32, kind="ExternalOutput").ap()
    xqT = nc.dram_tensor("xqT", [D, 1024], BF16, kind="ExternalOutput").ap()

    st = ExitStack()
    T = lambda n, s, d=F32: st.enter_context(nc.sbuf_tensor(n, s, d))
    yb = T("yb", [128, D]); xn1 = T("xn1", [128, D]); sq = T("sq", [128, D])
    scol = T("scol", [128, 1]); mucol = T("mucol", [128, 1]); vcol = T("vcol", [128, 1])
    sdcol = T("sdcol", [128, 1]); rscol = T("rscol", [128, 1])
    xfx = T("xfx", [128, D])
    iti = st.enter_context(nc.sbuf_tensor("iti", [128, D], I32))
    xfT = T("xfT", [128, 16 * 1024], BF16)
    ids = T("ids", [128, 128]); idsb = T("idsb", [128, 128], BF16)
    wraw = T("wraw", [128, D]); wbf = T("wbf", [128, D], BF16)
    scl = T("scl", [128, 3])
    fsb = T("fsb", [128, 1024]); isb = T("isb", [128, 1024]); gsb = T("gsb", [128, 1024])
    t1 = T("t1", [128, 1024]); ipre = T("ipre", [128, 1024])
    amax = T("amax", [128, 1]); qsc = T("qsc", [128, 1]); rqs = T("rqs", [128, 1])
    qt = T("qt", [128, D]); xqb = T("xqb", [128, D], BF16)
    xqbst = T("xqbst", [128, 128], BF16)
    ps_mm = st.enter_context(nc.psum_tensor("ps_mm", [128, 512], F32))
    ps_tr = st.enter_context(nc.psum_tensor("ps_tr", [128, 128], F32))
    ps_trb = st.enter_context(nc.psum_tensor("ps_trb", [128, 128], BF16))

    p = Prog(nc)
    p.dma(ids[:], ident[:])
    p.op("scalar", "copy", out=idsb[:], in_=ids[:])
    for nb in range(8):
        p.dma(yb[:], y0d[nb * 128:(nb + 1) * 128, :])
        _emit_plain_ln(p, yb[:], xn1[:], sq[:], scol[:], mucol[:], vcol[:], sdcol[:], rscol[:], D)
        p.op("vector", "tensor_scalar_mul", out=xfx[:], in0=xn1[:], scalar1=256.0)
        _round_to_int(p, xfx[:], iti[:])
        p.op("vector", "tensor_scalar_mul", out=xfx[:], in0=xfx[:], scalar1=1.0 / 256.0)
        for db in range(16):
            p.op("tensor", "transpose", out=ps_tr[:], in_=xfx[:, db * 128:(db + 1) * 128], identity=ids[:])
            p.op("scalar", "copy", out=xfT[:, db * 1024 + nb * 128: db * 1024 + nb * 128 + 128], in_=ps_tr[:])
        p.op("vector", "reduce_max", out=amax[:], in_=xn1[:], axis=AX.X, apply_absolute_value=True)
        p.op("vector", "tensor_scalar", out=qsc[:], in0=amax[:], scalar1=1.0 / 127.0, scalar2=1e-8,
             op0=ALU.mult, op1=ALU.max)
        p.op("vector", "reciprocal", out=rqs[:], in_=qsc[:])
        p.op("vector", "tensor_scalar_mul", out=qt[:], in0=xn1[:], scalar1=rqs[:])
        _round_to_int(p, qt[:], iti[:])
        p.op("vector", "tensor_scalar", out=qt[:], in0=qt[:], scalar1=-127.0, scalar2=127.0,
             op0=ALU.max, op1=ALU.min)
        p.op("vector", "tensor_scalar_mul", out=xqb[:], in0=qt[:], scalar1=qsc[:])
        for db in range(16):
            p.op("tensor", "transpose", out=ps_trb[:], in_=xqb[:, db * 128:(db + 1) * 128], identity=idsb[:])
            p.op("scalar", "copy", out=xqbst[:], in_=ps_trb[:])
            p.dma(xqT[db * 128:(db + 1) * 128, nb * 128:(nb + 1) * 128], xqbst[:])

    for (wd, sd_, od, kind) in [(wf, sf, fT, "f"), (wgg, sg, gsT, "g")]:
        for eb in range(16):
            p.dma(wraw[:].rearrange("p (db c) -> p db c", db=16),
                  wd.rearrange("(db p) c -> p db c", p=128)[:, :, eb * 128:(eb + 1) * 128])
            p.op("scalar", "copy", out=wbf[:], in_=wraw[:])
            p.dma(scl[:, 0:1], sd_[eb * 128:(eb + 1) * 128, :])
            for ch in range(2):
                for db in range(16):
                    p.op("tensor", "matmul", out=ps_mm[:, 0:512],
                         lhsT=wbf[:, db * 128:(db + 1) * 128],
                         rhs=xfT[:, db * 1024 + ch * 512: db * 1024 + ch * 512 + 512],
                         start=(db == 0), stop=(db == 15))
                if kind == "f":
                    p.op("scalar", "activation", out=fsb[:, ch * 512:(ch + 1) * 512], in_=ps_mm[:, 0:512],
                         func=AF.Sigmoid, scale=scl[:, 0:1])
                elif kind == "g":
                    p.op("scalar", "activation", out=gsb[:, ch * 512:(ch + 1) * 512], in_=ps_mm[:, 0:512],
                         func=AF.Silu, scale=scl[:, 0:1])
                else:
                    p.op("scalar", "activation", out=ipre[:, ch * 512:(ch + 1) * 512], in_=ps_mm[:, 0:512],
                         func=AF.Copy, scale=scl[:, 0:1])
            if kind == "f":
                p.dma(fT[eb * 128:(eb + 1) * 128, :], fsb[:])
            else:
                p.dma(gsT[eb * 128:(eb + 1) * 128, :], gsb[:])
    for eb in range(16):
        p.dma(fsb[:], fT[eb * 128:(eb + 1) * 128, :])
        p.dma(wraw[:].rearrange("p (db c) -> p db c", db=16),
              wi.rearrange("(db p) c -> p db c", p=128)[:, :, eb * 128:(eb + 1) * 128])
        p.op("scalar", "copy", out=wbf[:], in_=wraw[:])
        p.dma(scl[:, 0:1], si[eb * 128:(eb + 1) * 128, :])
        for ch in range(2):
            for db in range(16):
                p.op("tensor", "matmul", out=ps_mm[:, 0:512],
                     lhsT=wbf[:, db * 128:(db + 1) * 128],
                     rhs=xfT[:, db * 1024 + ch * 512: db * 1024 + ch * 512 + 512],
                     start=(db == 0), stop=(db == 15))
            p.op("scalar", "activation", out=ipre[:, ch * 512:(ch + 1) * 512], in_=ps_mm[:, 0:512],
                 func=AF.Copy, scale=scl[:, 0:1])
        p.op("vector", "tensor_scalar", out=t1[:], in0=fsb[:], scalar1=-1.0, scalar2=1.0,
             op0=ALU.mult, op1=ALU.add)
        p.op("vector", "tensor_mul", out=isb[:], in0=t1[:], in1=ipre[:])
        p.dma(iT[eb * 128:(eb + 1) * 128, :], isb[:])

    p.finalize(st)
    st.close()
    return nc


def run_l5(inp, y0_slabs):
    nc = build_l5()
    ident = np.eye(128, dtype=np.float32)
    im = []
    for c in range(NC):
        im.append({
            "y0d": np.ascontiguousarray(y0_slabs[c]),
            "wi": inp["L1_attn_w_i"].astype(np.float32),
            "wf": inp["L1_attn_w_f"].astype(np.float32),
            "wgg": inp["L1_attn_w_g"].astype(np.float32),
            "si": inp["L1_attn_s_i"].reshape(D, 1).astype(np.float32),
            "sf": inp["L1_attn_s_f"].reshape(D, 1).astype(np.float32),
            "sg": inp["L1_attn_s_g"].reshape(D, 1).astype(np.float32),
            "ident": ident,
        })
    res = run_bass_kernel_spmd(nc, im, core_ids=list(range(NC)))
    return res.results


def build_l6a():
    nc = bass.Bass("TRN2", target_bir_lowering=False)
    II = 5632
    xqTd = nc.dram_tensor("xqTd", [D, 1024], BF16, kind="ExternalInput").ap()
    w1T = nc.dram_tensor("w1T", [D, 2 * II], F32, kind="ExternalInput").ap()
    ones1 = nc.dram_tensor("ones1", [1, 128], F32, kind="ExternalInput").ap()
    hqd = nc.dram_tensor("hqd", [II, 1024], BF16, kind="ExternalOutput").ap()

    st = ExitStack()
    T = lambda n, s, d=F32: st.enter_context(nc.sbuf_tensor(n, s, d))
    xqT = T("xqT", [128, 16 * 1024], BF16)
    wtmp = T("wtmp", [128, 512])
    gpart = T("gpart", [1, 512])
    maccA = T("maccA", [1, 512]); maccB = T("maccB", [1, 512])
    scr = T("scr", [1, 512]); rscr = T("rscr", [1, 512])
    scb = T("scb", [128, 512]); rscb = T("rscb", [128, 512])
    qtmp = T("qtmp", [128, 512])
    iti = st.enter_context(nc.sbuf_tensor("iti", [128, 512], I32))
    wabs = T("wabs", [128, 512])
    habs = T("habs", [128, 1024])
    wqg = T("wqg", [128, 16 * 512], BF16)
    wqu = T("wqu", [128, 16 * 512], BF16)
    hbf = T("hbf", [128, 44 * 1024], BF16)
    s1 = T("s1", [128, 512])
    ones_s = T("ones_s", [1, 128])
    hpart = T("hpart", [1, 1024])
    hmA = T("hmA", [1, 1024]); hmB = T("hmB", [1, 1024])
    hsc = T("hsc", [1, 1024]); hrsc = T("hrsc", [1, 1024])
    psA = st.enter_context(nc.psum_tensor("psA", [128, 512], F32))
    psB = st.enter_context(nc.psum_tensor("psB", [128, 512], F32))
    psC = st.enter_context(nc.psum_tensor("psC", [128, 512], F32))

    p = Prog(nc)
    p.dma(ones_s[:], ones1[:])
    for db in range(16):
        p.dma(xqT[:, db * 1024:(db + 1) * 1024], xqTd[db * 128:(db + 1) * 128, :])

    w1v = w1T.rearrange("(db p) c -> p db c", p=128)

    def quant_chunk(base, wq_dst):
        curm, othm = maccA, maccB
        for db in range(16):
            p.dma(wtmp[:], w1v[:, db, base:base + 512])
            p.op("scalar", "activation", out=wabs[:], in_=wtmp[:], func=AF.Abs)
            p.op("gpsimd", "tensor_reduce", out=gpart[:], in_=wabs[:], axis=AX.C, op=ALU.max)
            if db == 0:
                p.op("vector", "tensor_copy", out=curm[:], in_=gpart[:])
            else:
                p.op("vector", "tensor_max", out=othm[:], in0=curm[:], in1=gpart[:])
                curm, othm = othm, curm
        p.op("vector", "tensor_scalar", out=scr[:], in0=curm[:], scalar1=1.0 / 127.0, scalar2=1e-8,
             op0=ALU.mult, op1=ALU.max)
        p.op("vector", "reciprocal", out=rscr[:], in_=scr[:])
        p.op("tensor", "matmul", out=psA[:, 0:512], lhsT=ones_s[:], rhs=scr[:], start=True, stop=True)
        p.op("vector", "tensor_copy", out=scb[:], in_=psA[:, 0:512])
        p.op("tensor", "matmul", out=psA[:, 0:512], lhsT=ones_s[:], rhs=rscr[:], start=True, stop=True)
        p.op("vector", "tensor_copy", out=rscb[:], in_=psA[:, 0:512])
        for db in range(16):
            p.dma(wtmp[:], w1v[:, db, base:base + 512])
            p.op("vector", "tensor_mul", out=qtmp[:], in0=wtmp[:], in1=rscb[:])
            _round_to_int(p, qtmp[:], iti[:])
            p.op("vector", "tensor_scalar", out=qtmp[:], in0=qtmp[:], scalar1=-127.0, scalar2=127.0,
                 op0=ALU.max, op1=ALU.min)
            p.op("vector", "tensor_mul", out=wq_dst[:, db * 512:(db + 1) * 512], in0=qtmp[:], in1=scb[:])

    for j in range(11):
        quant_chunk(j * 512, wqg)
        quant_chunk(II + j * 512, wqu)
        for o128 in range(4):
            iblk = j * 4 + o128
            for nch in range(2):
                for db in range(16):
                    p.op("tensor", "matmul", out=psB[:, 0:512],
                         lhsT=wqg[:, db * 512 + o128 * 128: db * 512 + o128 * 128 + 128],
                         rhs=xqT[:, db * 1024 + nch * 512: db * 1024 + nch * 512 + 512],
                         start=(db == 0), stop=(db == 15))
                for db in range(16):
                    p.op("tensor", "matmul", out=psC[:, 0:512],
                         lhsT=wqu[:, db * 512 + o128 * 128: db * 512 + o128 * 128 + 128],
                         rhs=xqT[:, db * 1024 + nch * 512: db * 1024 + nch * 512 + 512],
                         start=(db == 0), stop=(db == 15))
                p.op("scalar", "activation", out=s1[:], in_=psB[:, 0:512], func=AF.Silu)
                p.op("vector", "tensor_mul", out=hbf[:, iblk * 1024 + nch * 512: iblk * 1024 + nch * 512 + 512],
                     in0=s1[:], in1=psC[:, 0:512])

    curh, othh = hmA, hmB
    for ib in range(44):
        p.op("scalar", "activation", out=habs[:], in_=hbf[:, ib * 1024:(ib + 1) * 1024], func=AF.Abs)
        p.op("gpsimd", "tensor_reduce", out=hpart[:], in_=habs[:], axis=AX.C, op=ALU.max)
        if ib == 0:
            p.op("vector", "tensor_copy", out=curh[:], in_=hpart[:])
        else:
            p.op("vector", "tensor_max", out=othh[:], in0=curh[:], in1=hpart[:])
            curh, othh = othh, curh
    p.op("vector", "tensor_scalar", out=hsc[:], in0=curh[:], scalar1=1.0 / 127.0, scalar2=1e-8,
         op0=ALU.mult, op1=ALU.max)
    p.op("vector", "reciprocal", out=hrsc[:], in_=hsc[:])
    for ch in range(2):
        p.op("tensor", "matmul", out=psA[:, 0:512], lhsT=ones_s[:], rhs=hsc[0:1, ch * 512:(ch + 1) * 512],
             start=True, stop=True)
        p.op("vector", "tensor_copy", out=scb[:], in_=psA[:, 0:512])
        p.op("tensor", "matmul", out=psA[:, 0:512], lhsT=ones_s[:], rhs=hrsc[0:1, ch * 512:(ch + 1) * 512],
             start=True, stop=True)
        p.op("vector", "tensor_copy", out=rscb[:], in_=psA[:, 0:512])
        for ib in range(44):
            hsl = hbf[:, ib * 1024 + ch * 512: ib * 1024 + ch * 512 + 512]
            p.op("vector", "tensor_mul", out=qtmp[:], in0=hsl, in1=rscb[:])
            _round_to_int(p, qtmp[:], iti[:])
            p.op("vector", "tensor_scalar", out=qtmp[:], in0=qtmp[:], scalar1=-127.0, scalar2=127.0,
                 op0=ALU.max, op1=ALU.min)
            p.op("vector", "tensor_mul", out=hsl, in0=qtmp[:], in1=scb[:])
    for ib in range(44):
        p.dma(hqd[ib * 128:(ib + 1) * 128, :], hbf[:, ib * 1024:(ib + 1) * 1024])

    p.finalize(st)
    st.close()
    return nc


def run_l6a(inp, r5):
    nc = build_l6a()
    w1T = np.ascontiguousarray(inp["L1_ffn_w_gate_up"].T).astype(np.float32)
    ones1 = np.ones((1, 128), np.float32)
    im = []
    for c in range(NC):
        im.append({"xqTd": np.asarray(r5[c]["xqT"]), "w1T": w1T, "ones1": ones1})
    res = run_bass_kernel_spmd(nc, im, core_ids=list(range(NC)))
    return res.results


def build_l6b():
    nc = bass.Bass("TRN2", target_bir_lowering=False)
    II = 5632
    hqd = nc.dram_tensor("hqd", [II, 1024], BF16, kind="ExternalInput").ap()
    w2T = nc.dram_tensor("w2T", [II, D], F32, kind="ExternalInput").ap()
    ones1 = nc.dram_tensor("ones1", [1, 128], F32, kind="ExternalInput").ap()
    out2T = nc.dram_tensor("out2T", [D, 1024], F32, kind="ExternalOutput").ap()

    st = ExitStack()
    T = lambda n, s, d=F32: st.enter_context(nc.sbuf_tensor(n, s, d))
    hq = T("hq", [128, 44 * 1024], BF16)
    wtmp = T("wtmp", [128, 512])
    gpart = T("gpart", [1, 512])
    maccA = T("maccA", [1, 512]); maccB = T("maccB", [1, 512])
    scr = T("scr", [1, 512]); rscr = T("rscr", [1, 512])
    scb = T("scb", [128, 512]); rscb = T("rscb", [128, 512])
    qtmp = T("qtmp", [128, 512])
    iti = st.enter_context(nc.sbuf_tensor("iti", [128, 512], I32))
    wabs2 = T("wabs2", [128, 512])
    w2q = T("w2q", [128, 44 * 512], BF16)
    o_sb = T("o_sb", [128, 512])
    ones_s = T("ones_s", [1, 128])
    psA = st.enter_context(nc.psum_tensor("psA", [128, 512], F32))
    psB = st.enter_context(nc.psum_tensor("psB", [128, 512], F32))

    p = Prog(nc)
    p.dma(ones_s[:], ones1[:])
    for ib in range(44):
        p.dma(hq[:, ib * 1024:(ib + 1) * 1024], hqd[ib * 128:(ib + 1) * 128, :])
    w2v = w2T.rearrange("(ib p) c -> p ib c", p=128)

    for oc in range(4):
        base = oc * 512
        curm, othm = maccA, maccB
        for ib in range(44):
            p.dma(wtmp[:], w2v[:, ib, base:base + 512])
            p.op("scalar", "activation", out=wabs2[:], in_=wtmp[:], func=AF.Abs)
            p.op("gpsimd", "tensor_reduce", out=gpart[:], in_=wabs2[:], axis=AX.C, op=ALU.max)
            if ib == 0:
                p.op("vector", "tensor_copy", out=curm[:], in_=gpart[:])
            else:
                p.op("vector", "tensor_max", out=othm[:], in0=curm[:], in1=gpart[:])
                curm, othm = othm, curm
        p.op("vector", "tensor_scalar", out=scr[:], in0=curm[:], scalar1=1.0 / 127.0, scalar2=1e-8,
             op0=ALU.mult, op1=ALU.max)
        p.op("vector", "reciprocal", out=rscr[:], in_=scr[:])
        p.op("tensor", "matmul", out=psA[:, 0:512], lhsT=ones_s[:], rhs=scr[:], start=True, stop=True)
        p.op("vector", "tensor_copy", out=scb[:], in_=psA[:, 0:512])
        p.op("tensor", "matmul", out=psA[:, 0:512], lhsT=ones_s[:], rhs=rscr[:], start=True, stop=True)
        p.op("vector", "tensor_copy", out=rscb[:], in_=psA[:, 0:512])
        for ib in range(44):
            p.dma(wtmp[:], w2v[:, ib, base:base + 512])
            p.op("vector", "tensor_mul", out=qtmp[:], in0=wtmp[:], in1=rscb[:])
            _round_to_int(p, qtmp[:], iti[:])
            p.op("vector", "tensor_scalar", out=qtmp[:], in0=qtmp[:], scalar1=-127.0, scalar2=127.0,
                 op0=ALU.max, op1=ALU.min)
            p.op("vector", "tensor_mul", out=w2q[:, ib * 512:(ib + 1) * 512], in0=qtmp[:], in1=scb[:])
        for o128 in range(4):
            for nch in range(2):
                for ib in range(44):
                    p.op("tensor", "matmul", out=psB[:, 0:512],
                         lhsT=w2q[:, ib * 512 + o128 * 128: ib * 512 + o128 * 128 + 128],
                         rhs=hq[:, ib * 1024 + nch * 512: ib * 1024 + nch * 512 + 512],
                         start=(ib == 0), stop=(ib == 43))
                p.op("scalar", "copy", out=o_sb[:], in_=psB[:, 0:512])
                p.dma(out2T[base + o128 * 128: base + o128 * 128 + 128, nch * 512:(nch + 1) * 512], o_sb[:])

    p.finalize(st)
    st.close()
    return nc


def run_l6b(inp, r6a):
    nc = build_l6b()
    w2T = np.ascontiguousarray(inp["L1_ffn_w_down"].T).astype(np.float32)
    ones1 = np.ones((1, 128), np.float32)
    im = [{"hqd": np.asarray(r6a[c]["hqd"]), "w2T": w2T, "ones1": ones1} for c in range(NC)]
    res = run_bass_kernel_spmd(nc, im, core_ids=list(range(NC)))
    return res.results


def build_l7():
    nc = bass.Bass("TRN2", target_bir_lowering=False)
    f8 = nc.dram_tensor("f8", [8, 128, L], F32, kind="ExternalInput").ap()
    i8 = nc.dram_tensor("i8", [8, 128, L], F32, kind="ExternalInput").ap()
    h8 = nc.dram_tensor("h8", [8, 128, L], F32, kind="ExternalOutput").ap()
    st = ExitStack()
    T = lambda n, s, d=F32: st.enter_context(nc.sbuf_tensor(n, s, d))
    fs = T("fs", [128, L]); is_ = T("is_", [128, L]); hs = T("hs", [128, L])
    p = Prog(nc)
    for t in range(8):
        p.dma(fs[:], f8[t]); p.dma(is_[:], i8[t])
        p.op("vector", "tensor_tensor_scan", out=hs[:], data0=fs[:], data1=is_[:], initial=0.0,
             op0=ALU.mult, op1=ALU.add)
        p.dma(h8[t], hs[:])
    p.finalize(st)
    st.close()
    return nc


def run_l7(fT_full, iT_full):
    nc = build_l7()
    im = []
    for c in range(NC):
        fl = []; il = []
        for k in range(8):
            gidx = c * 8 + k
            b, cb = gidx // 16, gidx % 16
            fl.append(fT_full[b][cb * 128:(cb + 1) * 128])
            il.append(iT_full[b][cb * 128:(cb + 1) * 128])
        im.append({"f8": np.ascontiguousarray(np.stack(fl)), "i8": np.ascontiguousarray(np.stack(il))})
    res = run_bass_kernel_spmd(nc, im, core_ids=list(range(NC)))
    return res.results


def build_l8():
    nc = bass.Bass("TRN2", target_bir_lowering=False)
    hd = nc.dram_tensor("hd", [1024, D], F32, kind="ExternalInput").ap()
    gsd = nc.dram_tensor("gsd", [1024, D], F32, kind="ExternalInput").ap()
    y0d = nc.dram_tensor("y0d", [1024, D], F32, kind="ExternalInput").ap()
    o2d = nc.dram_tensor("o2d", [1024, D], F32, kind="ExternalInput").ap()
    woT = nc.dram_tensor("woT", [D, D], F32, kind="ExternalInput").ap()
    gnw = nc.dram_tensor("gnw", [128, D], F32, kind="ExternalInput").ap()
    sot = nc.dram_tensor("sot", [128, D], F32, kind="ExternalInput").ap()
    ident = nc.dram_tensor("ident", [128, 128], F32, kind="ExternalInput").ap()
    out = nc.dram_tensor("out", [1024, D], F32, kind="ExternalOutput").ap()

    st = ExitStack()
    T = lambda n, s, d=F32: st.enter_context(nc.sbuf_tensor(n, s, d))
    hb = T("hb", [128, D]); sq = T("sq", [128, D])
    ms = T("ms", [128, 1]); sd2 = T("sd2", [128, 1]); rs2 = T("rs2", [128, 1])
    o1 = T("o1", [128, D])
    gns = T("gns", [128, D]); sos = T("sos", [128, D])
    osoT = T("osoT", [128, 16 * 1024], BF16)
    ids = T("ids", [128, 128])
    ldr = T("ldr", [128, D]); wbf = T("wbf", [128, 16 * D], BF16)
    t1 = T("t1", [128, 512]); t2 = T("t2", [128, 512])
    iti = st.enter_context(nc.sbuf_tensor("iti", [128, 512], I32))
    yb = T("yb", [128, D]); ob2 = T("ob2", [128, D]); fin = T("fin", [128, D])
    ps_mm = st.enter_context(nc.psum_tensor("ps_mm", [128, 512], F32))
    ps_tr = st.enter_context(nc.psum_tensor("ps_tr", [128, 128], F32))

    p = Prog(nc)
    p.dma(ids[:], ident[:])
    p.dma(gns[:], gnw[:]); p.dma(sos[:], sot[:])
    for eb in range(16):
        p.dma(ldr[:], woT[eb * 128:(eb + 1) * 128, :])
        p.op("scalar", "copy", out=wbf[:, eb * D:(eb + 1) * D], in_=ldr[:])
    for nb in range(8):
        p.dma(hb[:], hd[nb * 128:(nb + 1) * 128, :])
        p.op("scalar", "activation", out=sq[:], in_=hb[:], func=AF.Square)
        p.op("vector", "reduce_sum", out=ms[:], in_=sq[:], axis=AX.X)
        p.op("vector", "tensor_scalar", out=ms[:], in0=ms[:], scalar1=1.0 / D, scalar2=1e-5,
             op0=ALU.mult, op1=ALU.add)
        p.op("scalar", "activation", out=sd2[:], in_=ms[:], func=AF.Sqrt)
        p.op("vector", "reciprocal", out=rs2[:], in_=sd2[:])
        p.op("vector", "tensor_scalar_mul", out=o1[:], in0=hb[:], scalar1=rs2[:])
        p.op("vector", "tensor_mul", out=o1[:], in0=o1[:], in1=gns[:])
        p.dma(ldr[:], gsd[nb * 128:(nb + 1) * 128, :])
        p.op("vector", "tensor_mul", out=o1[:], in0=o1[:], in1=ldr[:])
        p.op("vector", "tensor_mul", out=o1[:], in0=o1[:], in1=sos[:])
        for eb in range(16):
            p.op("tensor", "transpose", out=ps_tr[:], in_=o1[:, eb * 128:(eb + 1) * 128], identity=ids[:])
            p.op("scalar", "copy", out=osoT[:, eb * 1024 + nb * 128: eb * 1024 + nb * 128 + 128], in_=ps_tr[:])
    for nb in range(8):
        p.dma(yb[:], y0d[nb * 128:(nb + 1) * 128, :])
        p.dma(ob2[:], o2d[nb * 128:(nb + 1) * 128, :])
        for ch in range(4):
            for eb in range(16):
                p.op("tensor", "matmul", out=ps_mm[:, 0:512],
                     lhsT=osoT[:, eb * 1024 + nb * 128: eb * 1024 + nb * 128 + 128],
                     rhs=wbf[:, eb * D + ch * 512: eb * D + ch * 512 + 512],
                     start=(eb == 0), stop=(eb == 15))
            p.op("vector", "tensor_scalar_mul", out=t1[:], in0=ps_mm[:, 0:512], scalar1=256.0)
            _round_to_int(p, t1[:], iti[:])
            p.op("vector", "tensor_scalar_mul", out=t2[:], in0=t1[:], scalar1=1.0 / 256.0)
            p.op("vector", "tensor_add", out=fin[:, ch * 512:(ch + 1) * 512], in0=t2[:],
                 in1=yb[:, ch * 512:(ch + 1) * 512])
            p.op("vector", "tensor_add", out=fin[:, ch * 512:(ch + 1) * 512],
                 in0=fin[:, ch * 512:(ch + 1) * 512], in1=ob2[:, ch * 512:(ch + 1) * 512])
        p.dma(out[nb * 128:(nb + 1) * 128, :], fin[:])

    p.finalize(st)
    st.close()
    return nc


def run_l8(inp, h_slabs, gs_slabs, y0_slabs, o2_slabs):
    nc = build_l8()
    ident = np.eye(128, dtype=np.float32)
    woT = np.ascontiguousarray(inp["L1_attn_w_o"].T).astype(np.float32)
    gnw = np.broadcast_to(inp["L1_attn_g_norm_w"], (128, D)).astype(np.float32).copy()
    sot = np.broadcast_to(inp["L1_attn_s_o"], (128, D)).astype(np.float32).copy()
    im = []
    for c in range(NC):
        im.append({
            "hd": np.ascontiguousarray(h_slabs[c]),
            "gsd": np.ascontiguousarray(gs_slabs[c]),
            "y0d": np.ascontiguousarray(y0_slabs[c]),
            "o2d": np.ascontiguousarray(o2_slabs[c]),
            "woT": woT, "gnw": gnw, "sot": sot, "ident": ident,
        })
    res = run_bass_kernel_spmd(nc, im, core_ids=list(range(NC)))
    return [np.asarray(r["out"]) for r in res.results]


LNG = None
LNB = None


def kernel(**inp):
    global LNG, LNB
    LNG = inp["L0_attn_ln_g"].astype(np.float32)
    LNB = inp["L0_attn_ln_b"].astype(np.float32)

    r1 = run_l1(inp)
    og = run_l2(r1)
    oT_full = []
    for b in range(B):
        cols = np.empty((D, L), np.float32)
        for hg in range(2):
            c = 2 * b + hg
            for hh in range(2):
                o4 = og[c][hh]
                for hp in range(4):
                    gh = hg * 8 + hh * 4 + hp
                    cols[gh * 128:(gh + 1) * 128, :] = o4[hp].T
        oT_full.append(cols)
    xn_full = [np.asarray(r1[2 * b]["xn_out"]) for b in range(B)]

    r3 = run_l3(inp, oT_full, xn_full)
    x_flat = np.ascontiguousarray(inp["L0_attn_x"].reshape(B * L, D)).astype(np.float32)
    r4 = run_l4(inp, r3, xn_full, x_flat)
    y0_slabs = [np.asarray(r4[c]["y0"]) for c in range(NC)]

    r5 = run_l5(inp, y0_slabs)
    r6a = run_l6a(inp, r5)
    r6b = run_l6b(inp, r6a)
    fT_full = []; iT_full = []; gs_full = []; o2_full = []
    for b in range(B):
        f = np.concatenate([np.asarray(r5[2 * b]["fT"]), np.asarray(r5[2 * b + 1]["fT"])], axis=1)
        i_ = np.concatenate([np.asarray(r5[2 * b]["iT"]), np.asarray(r5[2 * b + 1]["iT"])], axis=1)
        g = np.concatenate([np.asarray(r5[2 * b]["gsT"]), np.asarray(r5[2 * b + 1]["gsT"])], axis=1)
        o2 = np.concatenate([np.asarray(r6b[2 * b]["out2T"]), np.asarray(r6b[2 * b + 1]["out2T"])], axis=1)
        fT_full.append(f); iT_full.append(i_); gs_full.append(g); o2_full.append(o2)
    r7 = run_l7(fT_full, iT_full)
    h_full = []
    for b in range(B):
        hb = np.empty((D, L), np.float32)
        for cb in range(16):
            gidx = b * 16 + cb
            c, k = gidx // 8, gidx % 8
            hb[cb * 128:(cb + 1) * 128] = np.asarray(r7[c]["h8"])[k]
        h_full.append(hb)
    h_slabs = []; gs_slabs = []; o2_slabs = []
    for c in range(NC):
        b, half = c // 2, c % 2
        tr = slice(half * 1024, half * 1024 + 1024)
        h_slabs.append(np.ascontiguousarray(h_full[b][:, tr].T))
        gs_slabs.append(np.ascontiguousarray(gs_full[b][:, tr].T))
        o2_slabs.append(np.ascontiguousarray(o2_full[b][:, tr].T))
    outs = run_l8(inp, h_slabs, gs_slabs, y0_slabs, o2_slabs)
    out = np.concatenate(outs, axis=0).reshape(B, L, D).astype(np.float32)
    return out
